# version 10
# baseline (speedup 1.0000x reference)
import math
from contextlib import ExitStack

import numpy as np
import ml_dtypes

import concourse.bass as bass
import concourse.mybir as mybir
from concourse.ap import AP
from concourse.bass_utils import run_bass_kernel_spmd

F32 = mybir.dt.float32
BF16 = mybir.dt.bfloat16
AF = mybir.ActivationFunctionType
ALU = mybir.AluOpType

D = 2048
HQ = 16
KVH = 4
DH = 128
INTER = 5632
NDC = D // 128
NIC = INTER // 128
CHUNK = 512
HALO = 1024
TOK = 4096
SHARD = HALO + TOK
NCH = TOK // CHUNK
SCALE = 1.0 / math.sqrt(DH)
EBIAS = -12.0
LNEPS = 1e-5
RMSEPS = 1e-6

GROUPS_LIMIT = None


class Tok:
    __slots__ = ("kind", "eng", "sem", "val")

    def __init__(self, kind, eng, sem, val):
        self.kind, self.eng, self.sem, self.val = kind, eng, sem, val


class Buf:
    __slots__ = ("name", "writers", "readers", "serial_reads")

    def __init__(self, name, serial_reads=False):
        self.name = name
        self.writers = []
        self.readers = []
        self.serial_reads = serial_reads


class Bld:
    ENGINES = ("t", "a", "v", "g", "s")

    def __init__(self):
        self.ops = {e: [] for e in self.ENGINES}
        self.cnt = {}
        self.mark = {}
        self.sem_names = []
        self.pending_t = []

    def sem(self, name):
        if name not in self.cnt:
            self.cnt[name] = 0
            self.sem_names.append(name)
        return name

    def _wait(self, eng, sem, val):
        if val <= 0:
            return
        if self.mark.get((eng, sem), -1) >= val:
            return
        self.mark[(eng, sem)] = val
        self.ops[eng].append(("w", sem, val))

    def _need(self, eng, tok):
        return not (tok.kind == "e" and tok.eng == eng == "t")

    def op(self, eng, fn, reads=(), writes=(), inc=None, inc_amt=1, dma=False):
        for b in reads:
            for t in b.writers:
                if self._need(eng, t):
                    self._wait(eng, t.sem, t.val)
            if b.serial_reads:
                for t in b.readers:
                    if t.sem is not None and t.eng != eng:
                        self._wait(eng, t.sem, t.val)
        for b in writes:
            for t in list(b.writers) + list(b.readers):
                if self._need(eng, t):
                    self._wait(eng, t.sem, t.val)
        if inc is not None:
            self.cnt[inc] += inc_amt
            tok = Tok("d" if dma else "e", eng, inc, self.cnt[inc])
            self.ops[eng].append(("i", fn, inc, inc_amt))
        else:
            tok = Tok("e", eng, None, 0)
            self.ops[eng].append(("o", fn))
        for b in reads:
            if tok.sem is not None:
                b.readers.append(tok)
            elif eng == "t":
                self.pending_t.append(("r", b))
        for b in writes:
            if tok.sem is not None:
                b.writers = [tok]
                b.readers = []
            else:
                b.writers.append(tok)
                if eng == "t":
                    self.pending_t.append(("w", b))
        if eng == "t" and inc is not None and not dma:
            for kind, b in self.pending_t:
                if kind == "r":
                    b.readers.append(tok)
                else:
                    if tok not in b.writers:
                        b.writers.append(tok)
            self.pending_t = []
        return tok

    def raw_wait(self, eng, sem, val):
        self._wait(eng, sem, val)


def build_program(limit_groups=None, stage="all"):
    nc = bass.Bass("TRN2", target_bir_lowering=False)

    x_d = nc.dram_tensor("x", [SHARD, D], F32, kind="ExternalInput").ap()
    wq_d = nc.dram_tensor("wqt", [D, D], F32, kind="ExternalInput").ap()
    wk_d = nc.dram_tensor("wkt", [D, KVH * DH], F32, kind="ExternalInput").ap()
    wv_d = nc.dram_tensor("wvt", [D, KVH * DH], F32, kind="ExternalInput").ap()
    wo_d = nc.dram_tensor("wot", [D, D], F32, kind="ExternalInput").ap()
    gw_d = nc.dram_tensor("gwt", [D, INTER], F32, kind="ExternalInput").ap()
    uw_d = nc.dram_tensor("uwt", [D, INTER], F32, kind="ExternalInput").ap()
    dw_d = nc.dram_tensor("dwt", [INTER, D], F32, kind="ExternalInput").ap()
    gs_d = nc.dram_tensor("gs2", [128, NIC], F32, kind="ExternalInput").ap()
    us_d = nc.dram_tensor("us2", [128, NIC], F32, kind="ExternalInput").ap()
    ds_d = nc.dram_tensor("ds2", [1, D], F32, kind="ExternalInput").ap()
    cos_d = nc.dram_tensor("costab", [128, SHARD], F32, kind="ExternalInput").ap()
    sin_d = nc.dram_tensor("sintab", [128, SHARD], F32, kind="ExternalInput").ap()
    em_d = nc.dram_tensor("embias", [128, NCH * 6], F32, kind="ExternalInput").ap()
    out_d = nc.dram_tensor("out", [TOK, D], F32, kind="ExternalOutput").ap()

    WMETA = {
        "wk": (wk_d, KVH * DH, D),
        "wv": (wv_d, KVH * DH, D),
        "wq": (wq_d, D, D),
        "wo": (wo_d, D, D),
        "gw": (gw_d, INTER, D),
        "uw": (uw_d, INTER, D),
        "dw": (dw_d, D, INTER),
    }
    wT = {}
    for n, (ap_, O, Din) in WMETA.items():
        wT[n] = nc.dram_tensor("wt_" + n, [Din, O], BF16, kind="Internal").ap()

    B = Bld()
    es = ExitStack()
    S = {}

    def sbt(name, shape, dt):
        S[name] = es.enter_context(nc.sbuf_tensor(name, shape, dt))
        return S[name]

    sbt("xin0", [128, D], F32); sbt("xin1", [128, D], F32)
    for i in range(4):
        sbt("xres%d" % i, [128, 512], F32)
    sbt("xn_tm", [128, D], BF16)
    sbt("xn_fm", [128, NDC * 512], BF16)
    sbt("kcache", [128, KVH * 2048], BF16)
    sbt("vcache", [128, 16 * 512], BF16)
    sbt("att", [128, 2 * 1024], BF16)
    sbt("ysb", [128, HQ * 512], BF16)
    sbt("hsb", [128, NIC * 512], BF16)
    S["qrdy"] = S["hsb"][:, 0:HQ * 512]
    S["dsrow"] = S["ysb"][:].bitcast(F32)[0:1, 0:D]
    sbt("w0", [128, 4096], BF16); sbt("w1", [128, 4096], BF16)
    sbt("w2", [128, 4096], BF16); sbt("w3", [128, 4096], BF16)
    sbt("qn", [128, 512], F32); sbt("qs", [128, 512], F32)
    sbt("s1", [128, 512], F32); sbt("abc", [128, 512], F32)
    sbt("q2", [128, 512], BF16)
    sbt("ctab0", [128, 512], F32); sbt("stab0", [128, 512], F32)
    sbt("ctab1", [128, 512], F32); sbt("stab1", [128, 512], F32)
    sbt("sg0", [128, 512], F32); sbt("sg1", [128, 512], F32)
    sbt("stats", [128, 24], F32)
    sbt("mv", [128, 2], F32)
    sbt("rstd", [128, 1], F32)
    sbt("nmr", [128, 1], F32)
    sbt("gsb", [128, NIC], F32); sbt("usb", [128, NIC], F32)
    sbt("emk", [128, NCH * 6], F32)
    sbt("zb", [128, 1], F32)
    sbt("ones_sq", [128, 128], BF16)
    sbt("csum", [128, 512], BF16)
    sbt("ones_f", [1, 128], F32)
    sbt("ident", [128, 128], BF16)

    sbt("dsbc", [128, D], BF16)
    ps_all = es.enter_context(nc.psum_tensor("ps_all", [128, 3584], F32))
    ps_tr = es.enter_context(nc.psum_tensor("ps_tr", [128, 1024], BF16))

    def psb(i):
        return ps_all[:, i * 512:(i + 1) * 512]


    for name in ("T", "A", "V", "G"):
        B.sem(name)
    for name in ("W0", "W1", "W2", "W3", "X0", "X1", "R0", "R1", "R2", "R3",
                 "TB0", "TB1", "TB2", "TB3", "O0", "O1", "O2", "O3",
                 "P0", "P1", "Q0", "Q1", "M"):
        B.sem(name)
    for w in WMETA:
        B.sem("C_" + w)

    bufs = {}

    def bf(name):
        if name not in bufs:
            bufs[name] = Buf(name)
        return bufs[name]

    bank = [bf("bank%d" % i) for i in range(7)]
    btr = bf("bank7")
    for _b in bank + [btr]:
        _b.serial_reads = True
    xin_bufs = [bf("xin0"), bf("xin1")]
    xres_bufs = [bf("xres%d" % i) for i in range(4)]
    tab_bufs = [bf("ctab0"), bf("stab0"), bf("ctab1"), bf("stab1")]
    wr_bufs = [bf("w0"), bf("w1"), bf("w2"), bf("w3")]
    b_xn_tm, b_xn_fm = bf("xn_tm"), bf("xn_fm")
    b_ysb, b_hsb = bf("ysb"), bf("hsb")
    b_qrdy = b_hsb
    b_att = [bf("att0"), bf("att1")]
    b_kc = [bf("kc%d" % i) for i in range(4)]
    b_vc = [bf("vc%d" % i) for i in range(4)]
    b_qn, b_qs, b_s1, b_abc, b_q2 = bf("qn"), bf("qs"), bf("s1"), bf("abc"), bf("q2")
    b_stats, b_mv, b_rstd, b_nmr = bf("stats"), bf("mv"), bf("rstd"), bf("nmr")
    b_csum = bf("csum")
    b_sg = [bf("sg0"), bf("sg1")]
    b_dsbc_pre = bf("dsbc_pre")
    c_ones_sq, c_ones_f, c_ident = bf("ones_sq"), bf("ones_f"), bf("ident")
    c_gsb, c_usb, c_emk = bf("gsb"), bf("usb"), bf("emk")
    c_dsrow = b_ysb

    W_SLOTS = ("w0", "w1", "w2", "w3")
    wri = [0]

    def wload(wname, offset, pattern, n_free, f32=False):
        j = wri[0] % 4
        wri[0] += 1
        slot, semn, b = W_SLOTS[j], "W%d" % j, wr_bufs[j]
        if f32:
            sa = AP(WMETA[wname][0].tensor, offset, pattern)
            B.op("g", lambda E, s=slot, sa_=sa, n=n_free: E.dma_start(
                out=S[s][:, 0:n], in_=sa_),
                reads=(), writes=(b,), inc=semn, inc_amt=16, dma=True)
            if wri[0] % 3 == 0:
                drain_cast(1)
        else:
            sa = AP(wT[wname].tensor, offset, pattern)
            B.op("s", lambda E, s=slot, sa_=sa, n=n_free: E.dma_start(
                out=S[s][:, 0:n], in_=sa_),
                reads=(bf("wtready_" + wname),), writes=(b,),
                inc=semn, inc_amt=16, dma=True)
        return S[slot], b

    def aload(dst_ap, src_ap, slot_buf, semn):
        B.op("a", lambda E, d=dst_ap, s=src_ap: E.dma_start(out=d, in_=s),
             reads=(), writes=(slot_buf,), inc=semn, inc_amt=16, dma=True)

    def mm(bi, lhsT_ap, rhs_ap, start, stop, reads, forceinc=False):
        return B.op("t", lambda E, o=psb(bi), l=lhsT_ap, r=rhs_ap, st=start,
                    sp=stop: E.matmul(o, l, r, start=st, stop=sp),
                    reads=reads, writes=(bank[bi],),
                    inc=("T" if (stop or forceinc) else None))

    c_zb = bf("zb")
    B.op("g", lambda E: E.memset(S["zb"][:], 0.0), writes=(c_zb,), inc="G")
    B.op("g", lambda E: E.memset(S["ones_sq"][:], 1.0), writes=(c_ones_sq,), inc="G")
    B.op("g", lambda E: E.memset(S["ones_f"][:], 1.0), writes=(c_ones_f,), inc="G")
    B.op("g", lambda E: E.memset(S["ident"][:], 0.0), writes=(c_ident,), inc="G")
    B.op("g", lambda E: E.affine_select(
        out=S["ident"][:], in_=S["ident"][:], compare_op=ALU.not_equal,
        fill=1.0, base=0, pattern=[[-1, 128]], channel_multiplier=1),
        reads=(c_ident,), writes=(c_ident,), inc="G")
    aload(S["gsb"][:], gs_d, c_gsb, "M")
    aload(S["usb"][:], us_d, c_usb, "M")
    aload(S["dsrow"][:], ds_d, c_dsrow, "M")
    aload(S["emk"][:], em_d, c_emk, "M")
    mfinal = B.cnt["M"]
    for cb_ in (c_gsb, c_usb, c_dsrow, c_emk):
        cb_.writers = [Tok("d", "a", "M", mfinal)]

    for j in range(4):
        B.op("t", lambda E, j_=j: E.matmul(
            psb(j_), S["ones_f"][:], S["dsrow"][0:1, j_ * 512:(j_ + 1) * 512],
            start=True, stop=True),
            reads=(c_dsrow, c_ones_f), writes=(bank[j],), inc="T")
        B.op("a", lambda E, j_=j: E.activation(
            out=S["dsbc"][:, j_ * 512:(j_ + 1) * 512], in_=psb(j_), func=AF.Copy),
            reads=(bank[j],), writes=(b_dsbc_pre,), inc="A")

    if stage == "const":
        _finish(nc, B, es, S)
        return nc

    CAST_ROWS = {"wk": 2048, "wv": 2048, "wq": 512, "wo": 512,
                 "gw": 256, "uw": 256, "dw": 704}

    cast_jobs = []

    def queue_cast(w):
        ap_, O, Din = WMETA[w]
        r = 0
        while r < Din:
            rr = min(CAST_ROWS[w], Din - r)
            cast_jobs.append((w, r, rr))
            r += rr

    def drain_cast(n=1):
        for _ in range(n):
            if not cast_jobs:
                return
            w, r0, rr = cast_jobs.pop(0)
            ap_ = WMETA[w][0]
            t = B.op("g", lambda E, a=ap_, wt_=wT[w], r_=r0, n=rr: E.dma_start(
                out=wt_[r_:r_ + n, :], in_=a[r_:r_ + n, :]),
                inc="C_" + w, inc_amt=16, dma=True)
            bf("wtready_" + w).writers.append(t)

    def emit_cast(w):
        queue_cast(w)
        drain_cast(len(cast_jobs))

    gi_state = [0]

    b_xnh = [bf("xnh0"), bf("xnh1")]

    def ln_stats(r0, ts):
        xs = "xin%d" % (ts % 2)
        xb = xin_bufs[ts % 2]
        aload(S[xs][:], AP(x_d.tensor, (r0 + 128 * ts) * D,
                           [[D, 128], [1, D]]), xb, "X%d" % (ts % 2))
        for k in range(4):
            B.op("v", lambda E, s=xs, k_=k: E.bn_stats(
                S["stats"][:, 6 * k_:6 * (k_ + 1)],
                S[s][:, 512 * k_:512 * (k_ + 1)]),
                reads=(xb,), writes=(b_stats,), inc="V")
        B.op("v", lambda E: E.bn_aggr(S["mv"][:], S["stats"][:]),
             reads=(b_stats,), writes=(b_mv,), inc="V")
        B.op("a", lambda E: E.activation(out=S["rstd"][:], in_=S["mv"][:, 1:2],
                                         func=AF.Ln, bias=np.float32(LNEPS)),
             reads=(b_mv,), writes=(b_rstd,), inc="A")
        B.op("a", lambda E: E.activation(out=S["rstd"][:], in_=S["rstd"][:],
                                         func=AF.Exp, scale=np.float32(-0.5)),
             reads=(b_rstd,), writes=(b_rstd,), inc="A")
        B.op("v", lambda E: E.scalar_tensor_tensor(
            out=S["nmr"][:], in0=S["mv"][:, 0:1], scalar=-1.0,
            in1=S["rstd"][:], op0=ALU.mult, op1=ALU.mult),
            reads=(b_mv, b_rstd), writes=(b_nmr,), inc="V")
        for hh in range(2):
            B.op("a", lambda E, s=xs, h_=hh: E.activation(
                out=S["xn_tm"][:, h_ * 1024:(h_ + 1) * 1024],
                in_=S[s][:, h_ * 1024:(h_ + 1) * 1024], func=AF.Identity,
                scale=S["rstd"][:], bias=S["nmr"][:]),
                reads=(xb, b_rstd, b_nmr), writes=(b_xnh[hh],), inc="A")

    def ln_tr(ts):
        for wv_ in range(2):
            for j in range(8):
                dc = 8 * wv_ + j
                B.op("t", lambda E, d=dc, j_=j: E.transpose(
                    ps_tr[:, j_ * 128:(j_ + 1) * 128],
                    S["xn_tm"][:, d * 128:(d + 1) * 128], S["ident"][:]),
                    reads=(b_xnh[wv_], c_ident), writes=(btr,),
                    inc=("T" if j == 7 else None))
            B.op("v", lambda E, w_=wv_, t_=ts: E.tensor_copy(
                out=AP(S["xn_fm"][:].tensor,
                       (8 * w_) * 512 + t_ * 128,
                       [[NDC * 512, 128], [512, 8], [1, 128]]),
                in_=ps_tr[:, 0:1024].rearrange("p (j f) -> p j f", j=8)),
                reads=(btr,), writes=(b_xn_fm,), inc="V")

    def ln_phase(r0):
        for ts in range(4):
            ln_stats(r0, ts)
            ln_tr(ts)

    def load_tab(r0):
        p = gi_state[0] % 2
        cslot, sslot = "ctab%d" % p, "stab%d" % p
        cb, sb_ = tab_bufs[2 * p], tab_bufs[2 * p + 1]
        aload(S[cslot][:], AP(cos_d.tensor, r0, [[SHARD, 128], [1, 512]]),
              cb, "TB%d" % (2 * p))
        aload(S[sslot][:], AP(sin_d.tensor, r0, [[SHARD, 128], [1, 512]]),
              sb_, "TB%d" % (2 * p + 1))
        return cslot, sslot, cb, sb_

    def qk_post(bi, dest_ap, dest_buf, cslot, sslot, cb, sb_):
        B.op("a", lambda E, b=bi: E.activation(
            out=S["q2"][:], in_=psb(b), func=AF.Square),
            reads=(bank[bi],), writes=(b_q2,), inc="A")
        B.op("t", lambda E: E.matmul(psb(5), S["ones_sq"][:], S["q2"][:],
                                     start=True, stop=True),
             reads=(b_q2, c_ones_sq), writes=(bank[5],), inc="T")
        B.op("a", lambda E: E.activation(out=S["abc"][:], in_=psb(5),
                                         func=AF.Ln, scale=np.float32(1.0 / DH),
                                         bias=np.float32(RMSEPS)),
             reads=(bank[5],), writes=(b_abc,), inc="A")
        B.op("a", lambda E: E.activation(out=S["abc"][:], in_=S["abc"][:],
                                         func=AF.Exp, scale=np.float32(-0.5)),
             reads=(b_abc,), writes=(b_abc,), inc="A")
        B.op("v", lambda E, b=bi: E.tensor_copy(out=S["qn"][:], in_=psb(b)),
             reads=(bank[bi],), writes=(b_qn,), inc="V")
        B.op("g", lambda E: E.tensor_copy(out=S["qs"][0:64, :],
                                          in_=S["qn"][64:128, :]),
             reads=(b_qn,), writes=(b_qs,), inc="G")
        B.op("g", lambda E: E.tensor_copy(out=S["qs"][64:128, :],
                                          in_=S["qn"][0:64, :]),
             reads=(b_qn,), writes=(b_qs,), inc="G")
        B.op("v", lambda E, c=cslot: E.tensor_mul(S["s1"][:], S["qn"][:], S[c][:]),
             reads=(b_qn, cb), writes=(b_s1,), inc="V")
        B.op("v", lambda E, s=sslot: E.tensor_mul(S["qs"][:], S["qs"][:], S[s][:]),
             reads=(b_qs, sb_), writes=(b_qs,), inc="V")
        B.op("v", lambda E: E.tensor_add(S["s1"][:], S["s1"][:], S["qs"][:]),
             reads=(b_s1, b_qs), writes=(b_s1,), inc="V")
        B.op("v", lambda E, d=dest_ap: E.tensor_mul(d, S["s1"][:], S["abc"][:]),
             reads=(b_s1, b_abc), writes=(dest_buf,), inc="V")

    def qkv_phase(r0, kv_only):
        cslot, sslot, cb, sb_ = load_tab(r0)
        if SUBQ < 1:
            return
        wbase = r0 % 2048
        kcr = b_kc[wbase // 512]
        vcr = b_vc[wbase // 512]
        kslots = [wload("wk", hf * 8 * 128 * 512,
                        [[512, 128], [128 * 512, 8], [1, 512]], 4096)
                  for hf in range(2)]
        if SUBQ < 2:
            return
        if SUBQ >= 3:
            for oc in range(KVH):
                for dc in range(NDC):
                    sl, slb = kslots[dc // 8]
                    mm(oc, sl[:, (dc % 8) * 512 + oc * 128:(dc % 8) * 512 + oc * 128 + 128],
                       S["xn_fm"][:, dc * 512:(dc + 1) * 512],
                       start=(dc == 0), stop=(dc == NDC - 1),
                       reads=(slb, b_xn_fm))
        if SUBQ >= 4:
            for oc in range(KVH):
                dest = S["kcache"][:, oc * 2048 + wbase: oc * 2048 + wbase + 512]
                qk_post(oc, dest, kcr, cslot, sslot, cb, sb_)
        if SUBQ < 5:
            return
        vslots = [wload("wv", hf * 8 * 128 * 512,
                        [[512, 128], [128 * 512, 8], [1, 512]], 4096)
                  for hf in range(2)]
        sbb = wbase // 128
        for tsv in range(4):
            for dc in range(NDC):
                sl, slb = vslots[dc // 8]
                mm(6, S["xn_fm"][:, dc * 512 + tsv * 128: dc * 512 + tsv * 128 + 128],
                   sl[:, (dc % 8) * 512:(dc % 8) * 512 + 512],
                   start=(dc == 0), stop=(dc == NDC - 1),
                   reads=(slb, b_xn_fm))
            B.op("v", lambda E, t_=tsv: E.tensor_copy(
                out=S["vcache"][:, (sbb + t_) * 512:(sbb + t_ + 1) * 512],
                in_=psb(6)),
                reads=(bank[6],), writes=(vcr,), inc="V")
        if kv_only or SUBQ < 6:
            return
        for qt in range(4):
            qslots = [wload("wq", hf * 8 * 128 * 2048 + qt * 512,
                            [[2048, 128], [128 * 2048, 8], [1, 512]],
                            4096) for hf in range(2)]
            for oci in range(4):
                for dc in range(NDC):
                    sl, slb = qslots[dc // 8]
                    mm(oci,
                       sl[:, (dc % 8) * 512 + oci * 128:(dc % 8) * 512 + oci * 128 + 128],
                       S["xn_fm"][:, dc * 512:(dc + 1) * 512],
                       start=(dc == 0), stop=(dc == NDC - 1),
                       reads=(slb, b_xn_fm))
            for oci in range(4):
                oc = 4 * qt + oci
                dest = S["qrdy"][:, oc * 512:(oc + 1) * 512]
                qk_post(oci, dest, b_qrdy, cslot, sslot, cb, sb_)

    def attn_phase(c):
        for h in range(HQ):
            kv = h // 4

            def kap(blk):
                sl = (512 * c + 128 * blk) % 2048
                return (S["kcache"][:, kv * 2048 + sl: kv * 2048 + sl + 128],
                        b_kc[sl // 512])

            def vap(blk):
                sl = (512 * c + 128 * blk) % 2048
                return (S["vcache"][:, (sl // 128) * 512 + kv * 128:
                                    (sl // 128) * 512 + kv * 128 + 128],
                        b_vc[sl // 512])

            def emit_scores(w):
                for j in range(2):
                    blk = 2 * w + j
                    ka, kb_ = kap(blk)
                    mm((w % 2) * 2 + j, ka, S["qrdy"][:, h * 512:(h + 1) * 512],
                       start=True, stop=True, reads=(kb_, b_qrdy))

            def emit_exp(w):
                p = w % 2
                B.op("a", lambda E, p_=p, col=c * 6 + w: E.activation(
                    out=S["att"][:, p_ * 1024:(p_ + 1) * 1024],
                    in_=ps_all[:, p_ * 1024:(p_ + 1) * 1024],
                    func=AF.Exp, scale=np.float32(SCALE),
                    bias=S["emk"][:, col:col + 1]),
                    reads=(bank[2 * p], bank[2 * p + 1], c_emk),
                    writes=(b_att[p],), inc="A")

            def emit_masks(w):
                p = w % 2
                for j in range(2):
                    blk = 2 * w + j
                    aap = S["att"][:, p * 1024 + j * 512: p * 1024 + (j + 1) * 512]
                    if blk == 0:
                        B.op("g", lambda E, a=aap: E.affine_select(
                            out=a, in_=a, compare_op=ALU.is_ge, fill=0.0,
                            base=-1, pattern=[[0, 512]], channel_multiplier=1),
                            reads=(b_att[p],), writes=(b_att[p],), inc="G")
                    if blk >= 8:
                        B.op("g", lambda E, a=aap, bb=blk: E.affine_select(
                            out=a, in_=a, compare_op=ALU.is_ge, fill=0.0,
                            base=-128 * (bb - 8), pattern=[[1, 512]],
                            channel_multiplier=-1),
                            reads=(b_att[p],), writes=(b_att[p],), inc="G")

            def emit_csum(w):
                p = w % 2
                a0 = S["att"][:, p * 1024: p * 1024 + 512]
                a1 = S["att"][:, p * 1024 + 512: (p + 1) * 1024]
                if w == 0:
                    B.op("v", lambda E, x0=a0, x1=a1: E.scalar_tensor_tensor(
                        out=S["qn"][:], in0=x0, scalar=1.0, in1=x1,
                        op0=ALU.mult, op1=ALU.add),
                        reads=(b_att[p],), writes=(b_qn,), inc="V")
                else:
                    B.op("v", lambda E, x0=a0, x1=a1: E.scalar_tensor_tensor(
                        out=S["qs"][:], in0=x0, scalar=1.0, in1=x1,
                        op0=ALU.mult, op1=ALU.add),
                        reads=(b_att[p],), writes=(b_qs,), inc="V")
                    B.op("v", lambda E: E.tensor_add(S["qn"][:], S["qn"][:],
                                                     S["qs"][:]),
                         reads=(b_qn, b_qs), writes=(b_qn,), inc="V")

            yb = 5 + (h % 2)

            def emit_av(w):
                p = w % 2
                for j in range(2):
                    idx = 2 * w + j
                    aap = S["att"][:, p * 1024 + j * 512: p * 1024 + (j + 1) * 512]
                    va, vb_ = vap(idx)
                    B.op("t", lambda E, a=aap, vv=va, st=(idx == 0),
                         sp=(idx == 11), o=yb:
                         E.matmul(psb(o), vv, a, start=st, stop=sp),
                         reads=(b_att[p], vb_), writes=(bank[yb],),
                         inc=("T" if idx == 11 else None))

            for w in range(6):
                emit_scores(w)
                emit_exp(w)
                emit_masks(w)
                emit_csum(w)
                if w >= 1:
                    emit_av(w - 1)
            emit_av(5)
            if h % 2 == 0:
                drain_cast(1)
            B.op("v", lambda E: E.tensor_copy(out=S["csum"][:], in_=S["qn"][:]),
                 reads=(b_qn,), writes=(b_csum,), inc="V")
            B.op("t", lambda E: E.matmul(psb(4), S["ones_sq"][:], S["csum"][:],
                                         start=True, stop=True),
                 reads=(b_csum, c_ones_sq), writes=(bank[4],), inc="T")
            B.op("v", lambda E: E.reciprocal(S["s1"][:], psb(4)),
                 reads=(bank[4],), writes=(b_s1,), inc="V")
            B.op("v", lambda E, h_=h, o=yb: E.tensor_mul(
                S["ysb"][:, h_ * 512:(h_ + 1) * 512], psb(o), S["s1"][:]),
                reads=(bank[yb], b_s1), writes=(b_ysb,), inc="V")

    def ffn_phase(f32=False):
        for ip in range(NIC // 2):
            gsl, gslb = wload("gw", ip * 256,
                              [[INTER, 128], [128 * INTER, 16], [1, 256]], 4096,
                              f32=f32)
            usl, uslb = wload("uw", ip * 256,
                              [[INTER, 128], [128 * INTER, 16], [1, 256]], 4096,
                              f32=f32)
            for icj in range(2):
                ic = 2 * ip + icj
                for dc in range(NDC):
                    mm(icj, gsl[:, dc * 256 + icj * 128: dc * 256 + icj * 128 + 128],
                       S["xn_fm"][:, dc * 512:(dc + 1) * 512],
                       start=(dc == 0), stop=(dc == NDC - 1),
                       reads=(gslb, b_xn_fm))
                for dc in range(NDC):
                    mm(2 + icj, usl[:, dc * 256 + icj * 128: dc * 256 + icj * 128 + 128],
                       S["xn_fm"][:, dc * 512:(dc + 1) * 512],
                       start=(dc == 0), stop=(dc == NDC - 1),
                       reads=(uslb, b_xn_fm))
                B.op("a", lambda E, b=icj, i=ic, sg="sg%d" % icj: E.activation(
                    out=S[sg][:], in_=psb(b), func=AF.Silu,
                    scale=S["gsb"][:, i:i + 1], bias=S["zb"][:]),
                    reads=(bank[icj], c_gsb, c_zb), writes=(b_sg[icj],), inc="A")
                B.op("v", lambda E, b=2 + icj, i=ic, sg="sg%d" % icj:
                     E.scalar_tensor_tensor(
                         out=S["hsb"][:, i * 512:(i + 1) * 512],
                         in0=psb(b), scalar=S["usb"][:, i:i + 1],
                         in1=S[sg][:], op0=ALU.mult, op1=ALU.mult),
                     reads=(bank[2 + icj], c_usb, b_sg[icj]),
                     writes=(b_hsb,), inc="V")

    def out_phase(c):
        r0 = HALO + c * CHUNK
        for og in range(4):
            wosl = [wload("wo", hf * 8 * 128 * 2048 + og * 512,
                          [[2048, 128], [128 * 2048, 8], [1, 512]],
                          4096, f32=(c == 0)) for hf in range(2)]
            for hdc in range(HQ):
                sl, slb = wosl[hdc // 8]
                for tsv in range(4):
                    mm(tsv,
                       S["ysb"][:, hdc * 512 + tsv * 128: hdc * 512 + tsv * 128 + 128],
                       sl[:, (hdc % 8) * 512:(hdc % 8) * 512 + 512],
                       start=(hdc == 0), stop=False,
                       reads=(slb, b_ysb),
                       forceinc=(hdc % 8 == 7 and tsv == 3))
            icb = 0
            while icb < NIC:
                nic_b = min(8, NIC - icb)
                sl, slb = wload("dw", icb * 128 * 2048 + og * 512,
                                [[2048, 128], [128 * 2048, nic_b], [1, 512]],
                                nic_b * 512, f32=(c == 0))
                for ici in range(nic_b):
                    B.op("v", lambda E, s_=sl, i=ici, og_=og: E.tensor_mul(
                        s_[:, i * 512:(i + 1) * 512], s_[:, i * 512:(i + 1) * 512],
                        S["dsbc"][:, og_ * 512:(og_ + 1) * 512]),
                        reads=(slb, b_dsbc_pre), writes=(slb,), inc="V")
                for ici in range(nic_b):
                    ic = icb + ici
                    for tsv in range(4):
                        mm(tsv,
                           S["hsb"][:, ic * 512 + tsv * 128: ic * 512 + tsv * 128 + 128],
                           sl[:, ici * 512:(ici + 1) * 512],
                           start=False, stop=(ic == NIC - 1),
                           reads=(slb, b_hsb),
                           forceinc=(ici == nic_b - 1 and tsv == 3))
                icb += nic_b
            for tsv in range(4):
                xr, xrb = "xres%d" % tsv, xres_bufs[tsv]
                aload(S[xr][:], AP(x_d.tensor, (r0 + 128 * tsv) * D + og * 512,
                                   [[D, 128], [1, 512]]), xrb, "R%d" % tsv)
                B.op("v", lambda E, x_=xr, b=tsv: E.tensor_add(
                    S[x_][:], psb(b), S[x_][:]),
                    reads=(bank[tsv],), writes=(xrb,), inc="V")
                B.op("a", lambda E, x_=xr, c_=c, t_=tsv, og_=og: E.dma_start(
                    out=AP(out_d.tensor, (512 * c_ + 128 * t_) * D + og_ * 512,
                           [[D, 128], [1, 512]]),
                    in_=S[x_][:]),
                    reads=(xrb,), writes=(), inc="O%d" % tsv,
                    inc_amt=16, dma=True)

    LV = {"pre": 1, "ln": 2, "qkv": 3, "attn": 4, "ffn": 5, "all": 6}[stage]
    ngroups = NCH if limit_groups is None else min(limit_groups, NCH)
    for _w in ("wk", "wv", "wq"):
        emit_cast(_w)
    if LV >= 2:
        for hi, r0 in enumerate((0, 512)):
            ln_phase(r0)
            if LV >= 3:
                qkv_phase(r0, kv_only=True)
            gi_state[0] += 1
        ln_phase(HALO)
        for c in range(ngroups):
            r0 = HALO + c * CHUNK
            if LV >= 3:
                qkv_phase(r0, kv_only=False)
            gi_state[0] += 1
            if c == 0:
                for _w in ("gw", "uw", "wo", "dw"):
                    queue_cast(_w)
            if c == 1:
                drain_cast(len(cast_jobs))
            if LV >= 4:
                attn_phase(c)
            nxt = LV >= 2 and c + 1 < ngroups
            if nxt:
                ln_stats(r0 + CHUNK, 0)
            if LV >= 5:
                ffn_phase(f32=(c == 0))
            if nxt:
                for ts in range(4):
                    if ts:
                        ln_stats(r0 + CHUNK, ts)
                    ln_tr(ts)
            if LV >= 6:
                out_phase(c)


    _finish(nc, B, es, S)
    return nc


def _finish(nc, B, es, S):
    for name in B.sem_names:
        if B.cnt[name] > 0:
            B.raw_wait("a", name, B.cnt[name])

    sems = {}
    for name in B.sem_names:
        sems[name] = es.enter_context(nc.semaphore(name))

    with nc.Block() as block:
        def runner(key):
            def _f(E):
                for item in B.ops[key]:
                    if item[0] == "w":
                        E.wait_ge(sems[item[1]], item[2])
                    elif item[0] == "i":
                        inst = item[1](E)
                        inst.then_inc(sems[item[2]], item[3])
                    else:
                        item[1](E)
            return _f
        block.tensor(runner("t"))
        block.scalar(runner("a"))
        block.vector(runner("v"))
        block.gpsimd(runner("g"))
        block.sync(runner("s"))

    es.close()


def _host_prep(inputs):
    x = np.asarray(inputs["L0_attn_x"])
    wq = np.asarray(inputs["L0_attn_wq"])
    wk = np.asarray(inputs["L0_attn_wk"])
    wv = np.asarray(inputs["L0_attn_wv"])
    wo = np.asarray(inputs["L0_attn_wo"])
    gw = np.asarray(inputs["L0_ffn_gate_w"])
    uw = np.asarray(inputs["L0_ffn_up_w"])
    dw = np.asarray(inputs["L0_ffn_down_w"])
    gs = np.asarray(inputs["L0_ffn_gate_s"])
    us = np.asarray(inputs["L0_ffn_up_s"])
    ds = np.asarray(inputs["L0_ffn_down_s"])

    perm = np.concatenate([np.arange(0, DH, 2), np.arange(1, DH, 2)])
    wq_p = np.ascontiguousarray(
        wq.reshape(HQ, DH, D)[:, perm, :].reshape(D, D))
    wk_p = np.ascontiguousarray(
        wk.reshape(KVH, DH, D)[:, perm, :].reshape(KVH * DH, D))

    gs2 = np.ascontiguousarray(gs.reshape(NIC, 128).T)
    us2 = np.ascontiguousarray(us.reshape(NIC, 128).T)
    ds2 = np.ascontiguousarray(ds.reshape(1, D))

    inv = (1.0 / (10000.0 ** (np.arange(0, DH, 2, dtype=np.float32) / DH))
           ).astype(np.float32)

    shared = dict(wqt=np.ascontiguousarray(wq_p.T),
                  wkt=np.ascontiguousarray(wk_p.T),
                  wvt=np.ascontiguousarray(wv.T),
                  wot=np.ascontiguousarray(wo.T),
                  gwt=np.ascontiguousarray(gw.T),
                  uwt=np.ascontiguousarray(uw.T),
                  dwt=np.ascontiguousarray(dw.T),
                  gs2=gs2, us2=us2, ds2=ds2)

    in_maps = []
    for core in range(8):
        b, s = core // 4, core % 4
        lo = s * TOK - HALO
        if s == 0:
            xs = np.concatenate(
                [np.zeros((HALO, D), np.float32), x[b, 0:TOK]], axis=0)
        else:
            xs = x[b, lo:lo + SHARD]
        xs = np.ascontiguousarray(xs, dtype=np.float32)

        pos = (np.arange(SHARD, dtype=np.float32) + np.float32(lo))
        freqs = pos[None, :] * inv[:, None]
        c64 = np.cos(freqs).astype(np.float32)
        s64 = np.sin(freqs).astype(np.float32)
        costab = np.ascontiguousarray(np.concatenate([c64, c64], axis=0))
        sintab = np.ascontiguousarray(np.concatenate([-s64, s64], axis=0))

        em = np.full((128, NCH * 6), EBIAS, np.float32)
        for c in range(NCH):
            for w in range(6):
                if s == 0 and ((c == 0 and w < 4) or (c == 1 and w < 2)):
                    em[:, c * 6 + w] = EBIAS - 1.0e5
        em = np.ascontiguousarray(em)

        m = dict(shared)
        m["x"] = xs
        m["costab"] = costab
        m["sintab"] = sintab
        m["embias"] = em
        in_maps.append(m)
    return in_maps


STAGE = "all"
SUBQ = 99
SUBP = 99


def kernel(**inputs):
    in_maps = _host_prep(inputs)
    nc = build_program(GROUPS_LIMIT, STAGE)
    res = run_bass_kernel_spmd(nc, in_maps, core_ids=list(range(8)))
    out = np.zeros((2, 16384, D), np.float32)
    for core in range(8):
        b, s = core // 4, core % 4
        out[b, s * TOK:(s + 1) * TOK, :] = res.results[core]["out"]
    return out


# version 11
# speedup vs baseline: 1.2652x; 1.2652x over previous
import math
from contextlib import ExitStack

import numpy as np
import ml_dtypes

import concourse.bass as bass
import concourse.mybir as mybir
from concourse.ap import AP
from concourse.bass_utils import run_bass_kernel_spmd

F32 = mybir.dt.float32
BF16 = mybir.dt.bfloat16
AF = mybir.ActivationFunctionType
ALU = mybir.AluOpType

D = 2048
HQ = 16
KVH = 4
DH = 128
INTER = 5632
NDC = D // 128
NIC = INTER // 128
CHUNK = 512
HALO = 1024
TOK = 4096
SHARD = HALO + TOK
NCH = TOK // CHUNK
SCALE = 1.0 / math.sqrt(DH)
EBIAS = -12.0
LNEPS = 1e-5
RMSEPS = 1e-6

GROUPS_LIMIT = None


class Tok:
    __slots__ = ("kind", "eng", "sem", "val")

    def __init__(self, kind, eng, sem, val):
        self.kind, self.eng, self.sem, self.val = kind, eng, sem, val


class Buf:
    __slots__ = ("name", "writers", "readers", "serial_reads")

    def __init__(self, name, serial_reads=False):
        self.name = name
        self.writers = []
        self.readers = []
        self.serial_reads = serial_reads


class Bld:
    ENGINES = ("t", "a", "v", "g", "s")

    def __init__(self):
        self.ops = {e: [] for e in self.ENGINES}
        self.cnt = {}
        self.mark = {}
        self.sem_names = []
        self.pending_t = []

    def sem(self, name):
        if name not in self.cnt:
            self.cnt[name] = 0
            self.sem_names.append(name)
        return name

    def _wait(self, eng, sem, val):
        if val <= 0:
            return
        if self.mark.get((eng, sem), -1) >= val:
            return
        self.mark[(eng, sem)] = val
        self.ops[eng].append(("w", sem, val))

    def _need(self, eng, tok):
        return not (tok.kind == "e" and tok.eng == eng == "t")

    def op(self, eng, fn, reads=(), writes=(), inc=None, inc_amt=1, dma=False):
        for b in reads:
            for t in b.writers:
                if self._need(eng, t):
                    self._wait(eng, t.sem, t.val)
            if b.serial_reads:
                for t in b.readers:
                    if t.sem is not None and t.eng != eng:
                        self._wait(eng, t.sem, t.val)
        for b in writes:
            for t in list(b.writers) + list(b.readers):
                if self._need(eng, t):
                    self._wait(eng, t.sem, t.val)
        if inc is not None:
            self.cnt[inc] += inc_amt
            tok = Tok("d" if dma else "e", eng, inc, self.cnt[inc])
            self.ops[eng].append(("i", fn, inc, inc_amt))
        else:
            tok = Tok("e", eng, None, 0)
            self.ops[eng].append(("o", fn))
        for b in reads:
            if tok.sem is not None:
                b.readers.append(tok)
            elif eng == "t":
                self.pending_t.append(("r", b))
        for b in writes:
            if tok.sem is not None:
                b.writers = [tok]
                b.readers = []
            else:
                b.writers.append(tok)
                if eng == "t":
                    self.pending_t.append(("w", b))
        if eng == "t" and inc is not None and not dma:
            for kind, b in self.pending_t:
                if kind == "r":
                    b.readers.append(tok)
                else:
                    if tok not in b.writers:
                        b.writers.append(tok)
            self.pending_t = []
        return tok

    def raw_wait(self, eng, sem, val):
        self._wait(eng, sem, val)


def build_program(limit_groups=None, stage="all"):
    nc = bass.Bass("TRN2", target_bir_lowering=False)

    x_d = nc.dram_tensor("x", [SHARD, D], F32, kind="ExternalInput").ap()
    wq_d = nc.dram_tensor("wqt", [D, D], F32, kind="ExternalInput").ap()
    wk_d = nc.dram_tensor("wkt", [D, KVH * DH], F32, kind="ExternalInput").ap()
    wv_d = nc.dram_tensor("wvt", [D, KVH * DH], F32, kind="ExternalInput").ap()
    wo_d = nc.dram_tensor("wot", [D, D], F32, kind="ExternalInput").ap()
    gw_d = nc.dram_tensor("gwt", [D, INTER], F32, kind="ExternalInput").ap()
    uw_d = nc.dram_tensor("uwt", [D, INTER], F32, kind="ExternalInput").ap()
    dw_d = nc.dram_tensor("dwt", [INTER, D], F32, kind="ExternalInput").ap()
    gs_d = nc.dram_tensor("gs2", [128, NIC], F32, kind="ExternalInput").ap()
    us_d = nc.dram_tensor("us2", [128, NIC], F32, kind="ExternalInput").ap()
    ds_d = nc.dram_tensor("ds2", [1, D], F32, kind="ExternalInput").ap()
    cos_d = nc.dram_tensor("costab", [128, SHARD], F32, kind="ExternalInput").ap()
    sin_d = nc.dram_tensor("sintab", [128, SHARD], F32, kind="ExternalInput").ap()
    em_d = nc.dram_tensor("embias", [128, NCH * 6], F32, kind="ExternalInput").ap()
    out_d = nc.dram_tensor("out", [TOK, D], F32, kind="ExternalOutput").ap()

    WMETA = {
        "wk": (wk_d, KVH * DH, D),
        "wv": (wv_d, KVH * DH, D),
        "wq": (wq_d, D, D),
        "wo": (wo_d, D, D),
        "gw": (gw_d, INTER, D),
        "uw": (uw_d, INTER, D),
        "dw": (dw_d, D, INTER),
    }
    wT = {}
    for n, (ap_, O, Din) in WMETA.items():
        wT[n] = nc.dram_tensor("wt_" + n, [Din, O], BF16, kind="Internal").ap()

    B = Bld()
    es = ExitStack()
    S = {}

    def sbt(name, shape, dt):
        S[name] = es.enter_context(nc.sbuf_tensor(name, shape, dt))
        return S[name]

    sbt("xin0", [128, D], F32); sbt("xin1", [128, D], F32)
    for i in range(4):
        sbt("xres%d" % i, [128, 512], F32)
    sbt("xn_tm", [128, D], BF16)
    sbt("xn_fm", [128, NDC * 512], BF16)
    sbt("kcache", [128, KVH * 2048], BF16)
    sbt("vcache", [128, 16 * 512], BF16)
    sbt("att", [128, 2 * 1024], BF16)
    sbt("ysb", [128, HQ * 512], BF16)
    sbt("hsb", [128, NIC * 512], BF16)
    S["qrdy"] = S["hsb"][:, 0:HQ * 512]
    S["dsrow"] = S["ysb"][:].bitcast(F32)[0:1, 0:D]
    sbt("w0", [128, 4096], BF16); sbt("w1", [128, 4096], BF16)
    sbt("w2", [128, 4096], BF16); sbt("w3", [128, 4096], BF16)
    sbt("qn", [128, 512], F32); sbt("qs", [128, 512], F32)
    sbt("s1", [128, 512], F32); sbt("abc", [128, 512], F32)
    sbt("q2", [128, 512], BF16)
    sbt("ctab0", [128, 512], F32); sbt("stab0", [128, 512], F32)
    sbt("ctab1", [128, 512], F32); sbt("stab1", [128, 512], F32)
    sbt("sg0", [128, 512], F32); sbt("sg1", [128, 512], F32)
    sbt("stats", [128, 24], F32)
    sbt("mv", [128, 2], F32)
    sbt("rstd", [128, 1], F32)
    sbt("nmr", [128, 1], F32)
    sbt("gsb", [128, NIC], F32); sbt("usb", [128, NIC], F32)
    sbt("emk", [128, NCH * 6], F32)
    sbt("zb", [128, 1], F32)
    sbt("ones_sq", [128, 128], BF16)
    sbt("csum", [128, 512], BF16)
    sbt("ones_f", [1, 128], F32)
    sbt("ident", [128, 128], BF16)

    sbt("dsbc", [128, D], BF16)
    ps_all = es.enter_context(nc.psum_tensor("ps_all", [128, 3584], F32))
    ps_tr = es.enter_context(nc.psum_tensor("ps_tr", [128, 1024], BF16))

    def psb(i):
        return ps_all[:, i * 512:(i + 1) * 512]


    for name in ("T", "A", "V", "G"):
        B.sem(name)
    for name in ("W0", "W1", "W2", "W3", "X0", "X1", "R0", "R1", "R2", "R3",
                 "TB0", "TB1", "TB2", "TB3", "O0", "O1", "O2", "O3",
                 "P0", "P1", "Q0", "Q1", "M"):
        B.sem(name)
    for w in WMETA:
        B.sem("C_" + w)

    bufs = {}

    def bf(name):
        if name not in bufs:
            bufs[name] = Buf(name)
        return bufs[name]

    bank = [bf("bank%d" % i) for i in range(7)]
    btr = bf("bank7")
    for _b in bank + [btr]:
        _b.serial_reads = True
    xin_bufs = [bf("xin0"), bf("xin1")]
    xres_bufs = [bf("xres%d" % i) for i in range(4)]
    tab_bufs = [bf("ctab0"), bf("stab0"), bf("ctab1"), bf("stab1")]
    wr_bufs = [bf("w0"), bf("w1"), bf("w2"), bf("w3")]
    b_xn_tm, b_xn_fm = bf("xn_tm"), bf("xn_fm")
    b_ysb, b_hsb = bf("ysb"), bf("hsb")
    b_qrdy = b_hsb
    b_att = [bf("att0"), bf("att1")]
    b_kc = [bf("kc%d" % i) for i in range(4)]
    b_vc = [bf("vc%d" % i) for i in range(4)]
    b_qn, b_qs, b_s1, b_abc, b_q2 = bf("qn"), bf("qs"), bf("s1"), bf("abc"), bf("q2")
    b_stats, b_mv, b_rstd, b_nmr = bf("stats"), bf("mv"), bf("rstd"), bf("nmr")
    b_csum = bf("csum")
    b_sg = [bf("sg0"), bf("sg1")]
    b_dsbc_pre = bf("dsbc_pre")
    c_ones_sq, c_ones_f, c_ident = bf("ones_sq"), bf("ones_f"), bf("ident")
    c_gsb, c_usb, c_emk = bf("gsb"), bf("usb"), bf("emk")
    c_dsrow = b_ysb

    W_SLOTS = ("w0", "w1", "w2", "w3")
    wri = [0]

    def wload(wname, offset, pattern, n_free, f32=False):
        j = wri[0] % 4
        wri[0] += 1
        slot, semn, b = W_SLOTS[j], "W%d" % j, wr_bufs[j]
        if f32:
            sa = AP(WMETA[wname][0].tensor, offset, pattern)
            B.op("g", lambda E, s=slot, sa_=sa, n=n_free: E.dma_start(
                out=S[s][:, 0:n], in_=sa_),
                reads=(), writes=(b,), inc=semn, inc_amt=16, dma=True)
        else:
            sa = AP(wT[wname].tensor, offset, pattern)
            B.op("s", lambda E, s=slot, sa_=sa, n=n_free: E.dma_start(
                out=S[s][:, 0:n], in_=sa_),
                reads=(bf("wtready_" + wname),), writes=(b,),
                inc=semn, inc_amt=16, dma=True)
        return S[slot], b

    def aload(dst_ap, src_ap, slot_buf, semn):
        B.op("a", lambda E, d=dst_ap, s=src_ap: E.dma_start(out=d, in_=s),
             reads=(), writes=(slot_buf,), inc=semn, inc_amt=16, dma=True)

    def mm(bi, lhsT_ap, rhs_ap, start, stop, reads, forceinc=False):
        return B.op("t", lambda E, o=psb(bi), l=lhsT_ap, r=rhs_ap, st=start,
                    sp=stop: E.matmul(o, l, r, start=st, stop=sp),
                    reads=reads, writes=(bank[bi],),
                    inc=("T" if (stop or forceinc) else None))

    c_zb = bf("zb")
    B.op("g", lambda E: E.memset(S["zb"][:], 0.0), writes=(c_zb,), inc="G")
    B.op("g", lambda E: E.memset(S["ones_sq"][:], 1.0), writes=(c_ones_sq,), inc="G")
    B.op("g", lambda E: E.memset(S["ones_f"][:], 1.0), writes=(c_ones_f,), inc="G")
    B.op("g", lambda E: E.memset(S["ident"][:], 0.0), writes=(c_ident,), inc="G")
    B.op("g", lambda E: E.affine_select(
        out=S["ident"][:], in_=S["ident"][:], compare_op=ALU.not_equal,
        fill=1.0, base=0, pattern=[[-1, 128]], channel_multiplier=1),
        reads=(c_ident,), writes=(c_ident,), inc="G")
    aload(S["gsb"][:], gs_d, c_gsb, "M")
    aload(S["usb"][:], us_d, c_usb, "M")
    aload(S["dsrow"][:], ds_d, c_dsrow, "M")
    aload(S["emk"][:], em_d, c_emk, "M")
    mfinal = B.cnt["M"]
    for cb_ in (c_gsb, c_usb, c_dsrow, c_emk):
        cb_.writers = [Tok("d", "a", "M", mfinal)]

    for j in range(4):
        B.op("t", lambda E, j_=j: E.matmul(
            psb(j_), S["ones_f"][:], S["dsrow"][0:1, j_ * 512:(j_ + 1) * 512],
            start=True, stop=True),
            reads=(c_dsrow, c_ones_f), writes=(bank[j],), inc="T")
        B.op("a", lambda E, j_=j: E.activation(
            out=S["dsbc"][:, j_ * 512:(j_ + 1) * 512], in_=psb(j_), func=AF.Copy),
            reads=(bank[j],), writes=(b_dsbc_pre,), inc="A")

    if stage == "const":
        _finish(nc, B, es, S)
        return nc

    CAST_ROWS = {"wk": 2048, "wv": 2048, "wq": 512, "wo": 512,
                 "gw": 256, "uw": 256, "dw": 704}

    def emit_cast(w):
        ap_, O, Din = WMETA[w]
        csem = "C_" + w
        toks = []
        r = 0
        while r < Din:
            rr = min(CAST_ROWS[w], Din - r)
            t = B.op("g", lambda E, a=ap_, wt_=wT[w], r0=r, n=rr: E.dma_start(
                out=wt_[r0:r0 + n, :], in_=a[r0:r0 + n, :]),
                inc=csem, inc_amt=16, dma=True)
            toks.append(t)
            r += rr
        rb = bf("wtready_" + w)
        rb.writers = list(toks)

    gi_state = [0]

    b_xnh = [bf("xnh0"), bf("xnh1")]

    def ln_stats(r0, ts):
        xs = "xin%d" % (ts % 2)
        xb = xin_bufs[ts % 2]
        aload(S[xs][:], AP(x_d.tensor, (r0 + 128 * ts) * D,
                           [[D, 128], [1, D]]), xb, "X%d" % (ts % 2))
        for k in range(4):
            B.op("v", lambda E, s=xs, k_=k: E.bn_stats(
                S["stats"][:, 6 * k_:6 * (k_ + 1)],
                S[s][:, 512 * k_:512 * (k_ + 1)]),
                reads=(xb,), writes=(b_stats,), inc="V")
        B.op("v", lambda E: E.bn_aggr(S["mv"][:], S["stats"][:]),
             reads=(b_stats,), writes=(b_mv,), inc="V")
        B.op("a", lambda E: E.activation(out=S["rstd"][:], in_=S["mv"][:, 1:2],
                                         func=AF.Ln, bias=np.float32(LNEPS)),
             reads=(b_mv,), writes=(b_rstd,), inc="A")
        B.op("a", lambda E: E.activation(out=S["rstd"][:], in_=S["rstd"][:],
                                         func=AF.Exp, scale=np.float32(-0.5)),
             reads=(b_rstd,), writes=(b_rstd,), inc="A")
        B.op("v", lambda E: E.scalar_tensor_tensor(
            out=S["nmr"][:], in0=S["mv"][:, 0:1], scalar=-1.0,
            in1=S["rstd"][:], op0=ALU.mult, op1=ALU.mult),
            reads=(b_mv, b_rstd), writes=(b_nmr,), inc="V")
        for hh in range(2):
            B.op("a", lambda E, s=xs, h_=hh: E.activation(
                out=S["xn_tm"][:, h_ * 1024:(h_ + 1) * 1024],
                in_=S[s][:, h_ * 1024:(h_ + 1) * 1024], func=AF.Identity,
                scale=S["rstd"][:], bias=S["nmr"][:]),
                reads=(xb, b_rstd, b_nmr), writes=(b_xnh[hh],), inc="A")

    def ln_tr(ts):
        for wv_ in range(2):
            for j in range(8):
                dc = 8 * wv_ + j
                B.op("t", lambda E, d=dc, j_=j: E.transpose(
                    ps_tr[:, j_ * 128:(j_ + 1) * 128],
                    S["xn_tm"][:, d * 128:(d + 1) * 128], S["ident"][:]),
                    reads=(b_xnh[wv_], c_ident), writes=(btr,),
                    inc=("T" if j == 7 else None))
            B.op("v", lambda E, w_=wv_, t_=ts: E.tensor_copy(
                out=AP(S["xn_fm"][:].tensor,
                       (8 * w_) * 512 + t_ * 128,
                       [[NDC * 512, 128], [512, 8], [1, 128]]),
                in_=ps_tr[:, 0:1024].rearrange("p (j f) -> p j f", j=8)),
                reads=(btr,), writes=(b_xn_fm,), inc="V")

    def ln_phase(r0):
        for ts in range(4):
            ln_stats(r0, ts)
            ln_tr(ts)

    def load_tab(r0):
        p = gi_state[0] % 2
        cslot, sslot = "ctab%d" % p, "stab%d" % p
        cb, sb_ = tab_bufs[2 * p], tab_bufs[2 * p + 1]
        aload(S[cslot][:], AP(cos_d.tensor, r0, [[SHARD, 128], [1, 512]]),
              cb, "TB%d" % (2 * p))
        aload(S[sslot][:], AP(sin_d.tensor, r0, [[SHARD, 128], [1, 512]]),
              sb_, "TB%d" % (2 * p + 1))
        return cslot, sslot, cb, sb_

    def qk_post(bi, dest_ap, dest_buf, cslot, sslot, cb, sb_):
        B.op("a", lambda E, b=bi: E.activation(
            out=S["q2"][:], in_=psb(b), func=AF.Square),
            reads=(bank[bi],), writes=(b_q2,), inc="A")
        B.op("t", lambda E: E.matmul(psb(5), S["ones_sq"][:], S["q2"][:],
                                     start=True, stop=True),
             reads=(b_q2, c_ones_sq), writes=(bank[5],), inc="T")
        B.op("a", lambda E: E.activation(out=S["abc"][:], in_=psb(5),
                                         func=AF.Ln, scale=np.float32(1.0 / DH),
                                         bias=np.float32(RMSEPS)),
             reads=(bank[5],), writes=(b_abc,), inc="A")
        B.op("a", lambda E: E.activation(out=S["abc"][:], in_=S["abc"][:],
                                         func=AF.Exp, scale=np.float32(-0.5)),
             reads=(b_abc,), writes=(b_abc,), inc="A")
        B.op("v", lambda E, b=bi: E.tensor_copy(out=S["qn"][:], in_=psb(b)),
             reads=(bank[bi],), writes=(b_qn,), inc="V")
        B.op("g", lambda E: E.tensor_copy(out=S["qs"][0:64, :],
                                          in_=S["qn"][64:128, :]),
             reads=(b_qn,), writes=(b_qs,), inc="G")
        B.op("g", lambda E: E.tensor_copy(out=S["qs"][64:128, :],
                                          in_=S["qn"][0:64, :]),
             reads=(b_qn,), writes=(b_qs,), inc="G")
        B.op("v", lambda E, c=cslot: E.tensor_mul(S["s1"][:], S["qn"][:], S[c][:]),
             reads=(b_qn, cb), writes=(b_s1,), inc="V")
        B.op("v", lambda E, s=sslot: E.tensor_mul(S["qs"][:], S["qs"][:], S[s][:]),
             reads=(b_qs, sb_), writes=(b_qs,), inc="V")
        B.op("v", lambda E: E.tensor_add(S["s1"][:], S["s1"][:], S["qs"][:]),
             reads=(b_s1, b_qs), writes=(b_s1,), inc="V")
        B.op("v", lambda E, d=dest_ap: E.tensor_mul(d, S["s1"][:], S["abc"][:]),
             reads=(b_s1, b_abc), writes=(dest_buf,), inc="V")

    def qkv_phase(r0, kv_only):
        cslot, sslot, cb, sb_ = load_tab(r0)
        if SUBQ < 1:
            return
        wbase = r0 % 2048
        kcr = b_kc[wbase // 512]
        vcr = b_vc[wbase // 512]
        kslots = [wload("wk", hf * 8 * 128 * 512,
                        [[512, 128], [128 * 512, 8], [1, 512]], 4096)
                  for hf in range(2)]
        if SUBQ < 2:
            return
        if SUBQ >= 3:
            for oc in range(KVH):
                for dc in range(NDC):
                    sl, slb = kslots[dc // 8]
                    mm(oc, sl[:, (dc % 8) * 512 + oc * 128:(dc % 8) * 512 + oc * 128 + 128],
                       S["xn_fm"][:, dc * 512:(dc + 1) * 512],
                       start=(dc == 0), stop=(dc == NDC - 1),
                       reads=(slb, b_xn_fm))
        if SUBQ >= 4:
            for oc in range(KVH):
                dest = S["kcache"][:, oc * 2048 + wbase: oc * 2048 + wbase + 512]
                qk_post(oc, dest, kcr, cslot, sslot, cb, sb_)
        if SUBQ < 5:
            return
        vslots = [wload("wv", hf * 8 * 128 * 512,
                        [[512, 128], [128 * 512, 8], [1, 512]], 4096)
                  for hf in range(2)]
        sbb = wbase // 128
        for tsv in range(4):
            for dc in range(NDC):
                sl, slb = vslots[dc // 8]
                mm(6, S["xn_fm"][:, dc * 512 + tsv * 128: dc * 512 + tsv * 128 + 128],
                   sl[:, (dc % 8) * 512:(dc % 8) * 512 + 512],
                   start=(dc == 0), stop=(dc == NDC - 1),
                   reads=(slb, b_xn_fm))
            B.op("v", lambda E, t_=tsv: E.tensor_copy(
                out=S["vcache"][:, (sbb + t_) * 512:(sbb + t_ + 1) * 512],
                in_=psb(6)),
                reads=(bank[6],), writes=(vcr,), inc="V")
        if kv_only or SUBQ < 6:
            return
        for qt in range(4):
            qslots = [wload("wq", hf * 8 * 128 * 2048 + qt * 512,
                            [[2048, 128], [128 * 2048, 8], [1, 512]],
                            4096) for hf in range(2)]
            for oci in range(4):
                for dc in range(NDC):
                    sl, slb = qslots[dc // 8]
                    mm(oci,
                       sl[:, (dc % 8) * 512 + oci * 128:(dc % 8) * 512 + oci * 128 + 128],
                       S["xn_fm"][:, dc * 512:(dc + 1) * 512],
                       start=(dc == 0), stop=(dc == NDC - 1),
                       reads=(slb, b_xn_fm))
            for oci in range(4):
                oc = 4 * qt + oci
                dest = S["qrdy"][:, oc * 512:(oc + 1) * 512]
                qk_post(oci, dest, b_qrdy, cslot, sslot, cb, sb_)

    def attn_phase(c):
        for h in range(HQ):
            kv = h // 4

            def kap(blk):
                sl = (512 * c + 128 * blk) % 2048
                return (S["kcache"][:, kv * 2048 + sl: kv * 2048 + sl + 128],
                        b_kc[sl // 512])

            def vap(blk):
                sl = (512 * c + 128 * blk) % 2048
                return (S["vcache"][:, (sl // 128) * 512 + kv * 128:
                                    (sl // 128) * 512 + kv * 128 + 128],
                        b_vc[sl // 512])

            def emit_scores(w):
                for j in range(2):
                    blk = 2 * w + j
                    ka, kb_ = kap(blk)
                    mm((w % 2) * 2 + j, ka, S["qrdy"][:, h * 512:(h + 1) * 512],
                       start=True, stop=True, reads=(kb_, b_qrdy))

            def emit_exp(w):
                p = w % 2
                B.op("a", lambda E, p_=p, col=c * 6 + w: E.activation(
                    out=S["att"][:, p_ * 1024:(p_ + 1) * 1024],
                    in_=ps_all[:, p_ * 1024:(p_ + 1) * 1024],
                    func=AF.Exp, scale=np.float32(SCALE),
                    bias=S["emk"][:, col:col + 1]),
                    reads=(bank[2 * p], bank[2 * p + 1], c_emk),
                    writes=(b_att[p],), inc="A")

            def emit_masks(w):
                p = w % 2
                for j in range(2):
                    blk = 2 * w + j
                    aap = S["att"][:, p * 1024 + j * 512: p * 1024 + (j + 1) * 512]
                    if blk == 0:
                        B.op("g", lambda E, a=aap: E.affine_select(
                            out=a, in_=a, compare_op=ALU.is_ge, fill=0.0,
                            base=-1, pattern=[[0, 512]], channel_multiplier=1),
                            reads=(b_att[p],), writes=(b_att[p],), inc="G")
                    if blk >= 8:
                        B.op("g", lambda E, a=aap, bb=blk: E.affine_select(
                            out=a, in_=a, compare_op=ALU.is_ge, fill=0.0,
                            base=-128 * (bb - 8), pattern=[[1, 512]],
                            channel_multiplier=-1),
                            reads=(b_att[p],), writes=(b_att[p],), inc="G")

            def emit_csum(w):
                p = w % 2
                a0 = S["att"][:, p * 1024: p * 1024 + 512]
                a1 = S["att"][:, p * 1024 + 512: (p + 1) * 1024]
                if w == 0:
                    B.op("v", lambda E, x0=a0, x1=a1: E.scalar_tensor_tensor(
                        out=S["qn"][:], in0=x0, scalar=1.0, in1=x1,
                        op0=ALU.mult, op1=ALU.add),
                        reads=(b_att[p],), writes=(b_qn,), inc="V")
                else:
                    B.op("v", lambda E, x0=a0, x1=a1: E.scalar_tensor_tensor(
                        out=S["qs"][:], in0=x0, scalar=1.0, in1=x1,
                        op0=ALU.mult, op1=ALU.add),
                        reads=(b_att[p],), writes=(b_qs,), inc="V")
                    B.op("v", lambda E: E.tensor_add(S["qn"][:], S["qn"][:],
                                                     S["qs"][:]),
                         reads=(b_qn, b_qs), writes=(b_qn,), inc="V")

            yb = 5 + (h % 2)

            def emit_av(w):
                p = w % 2
                for j in range(2):
                    idx = 2 * w + j
                    aap = S["att"][:, p * 1024 + j * 512: p * 1024 + (j + 1) * 512]
                    va, vb_ = vap(idx)
                    B.op("t", lambda E, a=aap, vv=va, st=(idx == 0),
                         sp=(idx == 11), o=yb:
                         E.matmul(psb(o), vv, a, start=st, stop=sp),
                         reads=(b_att[p], vb_), writes=(bank[yb],),
                         inc=("T" if idx == 11 else None))

            for w in range(6):
                emit_scores(w)
                emit_exp(w)
                emit_masks(w)
                emit_csum(w)
                if w >= 1:
                    emit_av(w - 1)
            emit_av(5)
            B.op("v", lambda E: E.tensor_copy(out=S["csum"][:], in_=S["qn"][:]),
                 reads=(b_qn,), writes=(b_csum,), inc="V")
            B.op("t", lambda E: E.matmul(psb(4), S["ones_sq"][:], S["csum"][:],
                                         start=True, stop=True),
                 reads=(b_csum, c_ones_sq), writes=(bank[4],), inc="T")
            B.op("v", lambda E: E.reciprocal(S["s1"][:], psb(4)),
                 reads=(bank[4],), writes=(b_s1,), inc="V")
            B.op("v", lambda E, h_=h, o=yb: E.tensor_mul(
                S["ysb"][:, h_ * 512:(h_ + 1) * 512], psb(o), S["s1"][:]),
                reads=(bank[yb], b_s1), writes=(b_ysb,), inc="V")

    def ffn_phase(f32=False):
        for ip in range(NIC // 2):
            gsl, gslb = wload("gw", ip * 256,
                              [[INTER, 128], [128 * INTER, 16], [1, 256]], 4096,
                              f32=f32)
            usl, uslb = wload("uw", ip * 256,
                              [[INTER, 128], [128 * INTER, 16], [1, 256]], 4096,
                              f32=f32)
            for icj in range(2):
                ic = 2 * ip + icj
                for dc in range(NDC):
                    mm(icj, gsl[:, dc * 256 + icj * 128: dc * 256 + icj * 128 + 128],
                       S["xn_fm"][:, dc * 512:(dc + 1) * 512],
                       start=(dc == 0), stop=(dc == NDC - 1),
                       reads=(gslb, b_xn_fm))
                for dc in range(NDC):
                    mm(2 + icj, usl[:, dc * 256 + icj * 128: dc * 256 + icj * 128 + 128],
                       S["xn_fm"][:, dc * 512:(dc + 1) * 512],
                       start=(dc == 0), stop=(dc == NDC - 1),
                       reads=(uslb, b_xn_fm))
                B.op("a", lambda E, b=icj, i=ic, sg="sg%d" % icj: E.activation(
                    out=S[sg][:], in_=psb(b), func=AF.Silu,
                    scale=S["gsb"][:, i:i + 1], bias=S["zb"][:]),
                    reads=(bank[icj], c_gsb, c_zb), writes=(b_sg[icj],), inc="A")
                B.op("v", lambda E, b=2 + icj, i=ic, sg="sg%d" % icj:
                     E.scalar_tensor_tensor(
                         out=S["hsb"][:, i * 512:(i + 1) * 512],
                         in0=psb(b), scalar=S["usb"][:, i:i + 1],
                         in1=S[sg][:], op0=ALU.mult, op1=ALU.mult),
                     reads=(bank[2 + icj], c_usb, b_sg[icj]),
                     writes=(b_hsb,), inc="V")

    def out_phase(c):
        r0 = HALO + c * CHUNK
        for og in range(4):
            for tsv in range(4):
                xr, xrb = "xres%d" % tsv, xres_bufs[tsv]
                aload(S[xr][:], AP(x_d.tensor, (r0 + 128 * tsv) * D + og * 512,
                                   [[D, 128], [1, 512]]), xrb, "R%d" % tsv)
            wosl = [wload("wo", hf * 8 * 128 * 2048 + og * 512,
                          [[2048, 128], [128 * 2048, 8], [1, 512]],
                          4096, f32=(c == 0)) for hf in range(2)]
            for hdc in range(HQ):
                sl, slb = wosl[hdc // 8]
                for tsv in range(4):
                    mm(tsv,
                       S["ysb"][:, hdc * 512 + tsv * 128: hdc * 512 + tsv * 128 + 128],
                       sl[:, (hdc % 8) * 512:(hdc % 8) * 512 + 512],
                       start=(hdc == 0), stop=False,
                       reads=(slb, b_ysb),
                       forceinc=(hdc % 8 == 7 and tsv == 3))
            icb = 0
            while icb < NIC:
                nic_b = min(8, NIC - icb)
                sl, slb = wload("dw", icb * 128 * 2048 + og * 512,
                                [[2048, 128], [128 * 2048, nic_b], [1, 512]],
                                nic_b * 512, f32=(c == 0))
                B.op("v", lambda E, s_=sl, n=nic_b, og_=og: E.tensor_mul(
                    s_[:, 0:n * 512], s_[:, 0:n * 512],
                    AP(S["dsbc"][:].tensor, og_ * 512,
                       [[2048, 128], [0, n], [1, 512]])),
                    reads=(slb, b_dsbc_pre), writes=(slb,), inc="V")
                for ici in range(nic_b):
                    ic = icb + ici
                    for tsv in range(4):
                        mm(tsv,
                           S["hsb"][:, ic * 512 + tsv * 128: ic * 512 + tsv * 128 + 128],
                           sl[:, ici * 512:(ici + 1) * 512],
                           start=False, stop=(ic == NIC - 1),
                           reads=(slb, b_hsb),
                           forceinc=(ici == nic_b - 1 and tsv == 3))
                icb += nic_b
            for tsv in range(4):
                xr, xrb = "xres%d" % tsv, xres_bufs[tsv]
                B.op("v", lambda E, x_=xr, b=tsv: E.tensor_add(
                    S[x_][:], psb(b), S[x_][:]),
                    reads=(bank[tsv],), writes=(xrb,), inc="V")
                B.op("a", lambda E, x_=xr, c_=c, t_=tsv, og_=og: E.dma_start(
                    out=AP(out_d.tensor, (512 * c_ + 128 * t_) * D + og_ * 512,
                           [[D, 128], [1, 512]]),
                    in_=S[x_][:]),
                    reads=(xrb,), writes=(), inc="O%d" % tsv,
                    inc_amt=16, dma=True)

    LV = {"pre": 1, "ln": 2, "qkv": 3, "attn": 4, "ffn": 5, "all": 6}[stage]
    ngroups = NCH if limit_groups is None else min(limit_groups, NCH)
    for _w in ("wk", "wv", "wq"):
        emit_cast(_w)
    if LV >= 2:
        for hi, r0 in enumerate((0, 512)):
            ln_phase(r0)
            if LV >= 3:
                qkv_phase(r0, kv_only=True)
            gi_state[0] += 1
        ln_phase(HALO)
        for c in range(ngroups):
            r0 = HALO + c * CHUNK
            if LV >= 3:
                qkv_phase(r0, kv_only=False)
            gi_state[0] += 1
            if c == 1:
                for _w in ("gw", "uw", "wo", "dw"):
                    emit_cast(_w)
            if LV >= 4:
                attn_phase(c)
            nxt = LV >= 2 and c + 1 < ngroups
            if nxt:
                ln_stats(r0 + CHUNK, 0)
            if LV >= 5:
                ffn_phase(f32=(c == 0))
            if nxt:
                for ts in range(4):
                    if ts:
                        ln_stats(r0 + CHUNK, ts)
                    ln_tr(ts)
            if LV >= 6:
                out_phase(c)


    _finish(nc, B, es, S)
    return nc


def _finish(nc, B, es, S):
    for name in B.sem_names:
        if B.cnt[name] > 0:
            B.raw_wait("a", name, B.cnt[name])

    sems = {}
    for name in B.sem_names:
        sems[name] = es.enter_context(nc.semaphore(name))

    with nc.Block() as block:
        def runner(key):
            def _f(E):
                for item in B.ops[key]:
                    if item[0] == "w":
                        E.wait_ge(sems[item[1]], item[2])
                    elif item[0] == "i":
                        inst = item[1](E)
                        inst.then_inc(sems[item[2]], item[3])
                    else:
                        item[1](E)
            return _f
        block.tensor(runner("t"))
        block.scalar(runner("a"))
        block.vector(runner("v"))
        block.gpsimd(runner("g"))
        block.sync(runner("s"))

    es.close()


def _host_prep(inputs):
    x = np.asarray(inputs["L0_attn_x"])
    wq = np.asarray(inputs["L0_attn_wq"])
    wk = np.asarray(inputs["L0_attn_wk"])
    wv = np.asarray(inputs["L0_attn_wv"])
    wo = np.asarray(inputs["L0_attn_wo"])
    gw = np.asarray(inputs["L0_ffn_gate_w"])
    uw = np.asarray(inputs["L0_ffn_up_w"])
    dw = np.asarray(inputs["L0_ffn_down_w"])
    gs = np.asarray(inputs["L0_ffn_gate_s"])
    us = np.asarray(inputs["L0_ffn_up_s"])
    ds = np.asarray(inputs["L0_ffn_down_s"])

    perm = np.concatenate([np.arange(0, DH, 2), np.arange(1, DH, 2)])
    wq_p = np.ascontiguousarray(
        wq.reshape(HQ, DH, D)[:, perm, :].reshape(D, D))
    wk_p = np.ascontiguousarray(
        wk.reshape(KVH, DH, D)[:, perm, :].reshape(KVH * DH, D))

    gs2 = np.ascontiguousarray(gs.reshape(NIC, 128).T)
    us2 = np.ascontiguousarray(us.reshape(NIC, 128).T)
    ds2 = np.ascontiguousarray(ds.reshape(1, D))

    inv = (1.0 / (10000.0 ** (np.arange(0, DH, 2, dtype=np.float32) / DH))
           ).astype(np.float32)

    shared = dict(wqt=np.ascontiguousarray(wq_p.T),
                  wkt=np.ascontiguousarray(wk_p.T),
                  wvt=np.ascontiguousarray(wv.T),
                  wot=np.ascontiguousarray(wo.T),
                  gwt=np.ascontiguousarray(gw.T),
                  uwt=np.ascontiguousarray(uw.T),
                  dwt=np.ascontiguousarray(dw.T),
                  gs2=gs2, us2=us2, ds2=ds2)

    in_maps = []
    for core in range(8):
        b, s = core // 4, core % 4
        lo = s * TOK - HALO
        if s == 0:
            xs = np.concatenate(
                [np.zeros((HALO, D), np.float32), x[b, 0:TOK]], axis=0)
        else:
            xs = x[b, lo:lo + SHARD]
        xs = np.ascontiguousarray(xs, dtype=np.float32)

        pos = (np.arange(SHARD, dtype=np.float32) + np.float32(lo))
        freqs = pos[None, :] * inv[:, None]
        c64 = np.cos(freqs).astype(np.float32)
        s64 = np.sin(freqs).astype(np.float32)
        costab = np.ascontiguousarray(np.concatenate([c64, c64], axis=0))
        sintab = np.ascontiguousarray(np.concatenate([-s64, s64], axis=0))

        em = np.full((128, NCH * 6), EBIAS, np.float32)
        for c in range(NCH):
            for w in range(6):
                if s == 0 and ((c == 0 and w < 4) or (c == 1 and w < 2)):
                    em[:, c * 6 + w] = EBIAS - 1.0e5
        em = np.ascontiguousarray(em)

        m = dict(shared)
        m["x"] = xs
        m["costab"] = costab
        m["sintab"] = sintab
        m["embias"] = em
        in_maps.append(m)
    return in_maps


STAGE = "all"
SUBQ = 99
SUBP = 99


def kernel(**inputs):
    in_maps = _host_prep(inputs)
    nc = build_program(GROUPS_LIMIT, STAGE)
    res = run_bass_kernel_spmd(nc, in_maps, core_ids=list(range(8)))
    out = np.zeros((2, 16384, D), np.float32)
    for core in range(8):
        b, s = core // 4, core % 4
        out[b, s * TOK:(s + 1) * TOK, :] = res.results[core]["out"]
    return out
